# version 01
# baseline (speedup 1.0000x reference)
from concourse.bass_utils import run_bass_kernel_spmd

import numpy as np
import concourse.bass as bass
import concourse.mybir as mybir

F32 = mybir.dt.float32
BF16 = mybir.dt.bfloat16
I32 = mybir.dt.int32
AF = mybir.ActivationFunctionType
ALU = mybir.AluOpType
AX = mybir.AxisListType

ENGINES = ["sync", "tensor", "vector", "scalar", "gpsimd"]

D = 4096
DCH = 32
TOKB = 8
TOK = 1024


class Sem:
    def __init__(self, handle, name):
        self.h = handle
        self.name = name
        self.n = 0

    def bump(self, k=1):
        self.n += k
        return (self.h, k)


class EngQ:

    def __init__(self, name):
        self.name = name
        self.ops = []

    def op(self, method, *a, inc=None, **kw):
        incs = inc if isinstance(inc, list) else ([inc] if inc else [])
        self.ops.append((method, a, kw, incs))

    def wait(self, sem, val):
        assert val <= sem.n, f"wait {sem.name}>={val} but only {sem.n} registered"
        self.ops.append(("wait_ge", (sem.h, val), {}, []))

    def wait_future(self, sem, val):
        self.ops.append(("wait_ge", (sem.h, val), {}, []))

    def guard(self, sem):
        m, a, kw, incs = self.ops[-1]
        assert m != "wait_ge"
        incs.append((sem.h, 1))
        sem.n += 1
        self.ops.append(("wait_ge", (sem.h, sem.n), {}, []))

    def replay(self, eng):
        for oi, (method, a, kw, incs) in enumerate(self.ops):
            if method == "_switch_core":
                fn = a[0]
                for arm in eng.switch_core_id(n=128):
                    fn(eng, int(arm))
                continue
            try:
                ins = getattr(eng, method)(*a, **kw)
            except Exception as e:
                raise RuntimeError(
                    f"replay fail {self.name} op#{oi} {method} "
                    f"shapes={[getattr(x, 'shape', None) for x in a]}") from e
            for (h, k) in incs:
                ins = ins.then_inc(h, k)

    def switch_core(self, fn):
        self.ops.append(("_switch_core", (fn,), {}, []))


class Ctx:
    def __init__(self, nc):
        self.nc = nc
        self.q = {e: EngQ(e) for e in ENGINES}
        self.sems = {}
        self._guards = []
        self.sb_off = 0
        self.sb_cap = 212480
        self.arena = None
        self.psum = []
        self.named = {}

    def setup(self):
        g = self.nc.sbuf_tensor("arena", [128, self.sb_cap // 4], F32)
        self.arena = g.__enter__()
        self._guards.append(g)
        for i in range(8):
            g = self.nc.psum_tensor(f"bank{i}", [128, 512], F32)
            self.psum.append(g.__enter__().ap())
            self._guards.append(g)

    def sem(self, name):
        if name not in self.sems:
            g = self.nc.semaphore(name)
            self._guards.append(g)
            self.sems[name] = Sem(g.__enter__(), name)
        return self.sems[name]

    def sbuf(self, name, cols, dtype, partitions=128):
        esz = 2 if dtype in (BF16,) else 4
        off = (self.sb_off + 3) // 4 * 4
        nbytes = cols * esz
        assert off + nbytes <= self.sb_cap, f"SBUF overflow at {name}: {(off + nbytes)/1024:.1f}K"
        ap = self.arena.ap()[0:partitions, off // 4:(off + nbytes + 3) // 4]
        if dtype != F32:
            ap = ap.bitcast(dtype)
        ap = ap[:, 0:cols]
        self.sb_off = off + nbytes
        self.named[name] = ap
        self.named[name + "_off"] = off
        return ap

    def at(self, off_bytes, cols, dtype, partitions=128):
        esz = 2 if dtype in (BF16,) else 4
        assert off_bytes % 4 == 0
        ap = self.arena.ap()[0:partitions, off_bytes // 4:(off_bytes + cols * esz + 3) // 4]
        if dtype != F32:
            ap = ap.bitcast(dtype)
        return ap[:, 0:cols]

    def close(self):
        for g in reversed(self._guards):
            g.__exit__(None, None, None)

    def emit(self):
        with self.nc.Block() as blk:
            for e in ENGINES:
                qq = self.q[e]
                getattr(blk, e)(lambda eng, qq=qq: qq.replay(eng))


def bank_bf16(ctx, i):
    return ctx.psum[i].bitcast(BF16)


def build_prelude(ctx):
    q = ctx.q
    s = ctx.sem("pre")
    ident = ctx.sbuf("ident", 128, BF16)
    ones_r = ctx.sbuf("ones_r", 128, BF16, partitions=1)
    ones_c = ctx.sbuf("ones_c", 1, BF16)
    ones16 = ctx.sbuf("ones16", 16, F32, partitions=1)
    identf = ctx.sbuf("identf", 128, F32)
    onesf_r = ctx.sbuf("onesf_r", 128, F32, partitions=1)
    eps_c = ctx.sbuf("eps_c", 1, F32)
    eps6_c = ctx.sbuf("eps6_c", 1, F32)
    q["gpsimd"].op("memset", eps_c, 1e-5)
    q["gpsimd"].op("memset", eps6_c, 1e-6)
    q["gpsimd"].op("memset", ident, 0.0)
    q["gpsimd"].op("affine_select", out=ident, in_=ident, compare_op=ALU.not_equal,
                   fill=1.0, base=0, pattern=[[-1, 128]], channel_multiplier=1)
    q["gpsimd"].op("memset", ones_r[0:1, :], 1.0)
    q["gpsimd"].op("memset", ones_c, 1.0)
    q["gpsimd"].op("memset", identf, 0.0)
    q["gpsimd"].op("affine_select", out=identf, in_=identf, compare_op=ALU.not_equal,
                   fill=1.0, base=0, pattern=[[-1, 128]], channel_multiplier=1)
    q["gpsimd"].op("memset", onesf_r[0:1, :], 1.0)
    q["gpsimd"].op("memset", ones16[0:1, :], 1.0, inc=s.bump())
    return dict(ident=ident, ones_r=ones_r[0:1, :], ones_c=ones_c, ones16=ones16[0:1, :],
                eps_c=eps_c, eps6_c=eps6_c, identf=identf, onesf_r=onesf_r[0:1, :])


def build_ln_xnfm(ctx, pre, x_hbm, xn_fm, bufs, tag, tb_list=None):
    q = ctx.q
    s_ld = ctx.sem("ld_" + tag)
    s_dv = ctx.sem("dv_" + tag)
    s_ac = ctx.sem("ac_" + tag)
    s_pe = ctx.sem("pe_" + tag)
    s_tr = ctx.sem("tr_" + tag)
    ident = pre["ident"]
    if tb_list is None:
        tb_list = range(TOKB)

    for i, tb in enumerate(tb_list):
        par = i % 2
        xt = bufs["xtile"][i % len(bufs["xtile"])]
        xn = bufs["xnbf"][i % len(bufs["xnbf"])]
        st = bufs["st"][par]
        nxt = len(bufs["xtile"])
        if i >= nxt:
            q["sync"].wait(s_dv, 2 * (i - nxt) + 2)
        q["sync"].op("dma_start", out=xt, in_=x_hbm[tb * 128:(tb + 1) * 128, :],
                     inc=s_ld.bump(16))
        q["vector"].wait(s_ld, 16 * (i + 1))
        if i >= 2:
            q["vector"].wait(s_ac, i - 1)
        q["vector"].op("reduce_sum", st[:, 0:1], xt, axis=AX.X)
        q["vector"].op("scalar_tensor_tensor", bufs["xsq"], xt, 1.0, xt,
                       op0=ALU.mult, op1=ALU.mult, accum_out=st[:, 1:2])
        q["vector"].op("tensor_scalar", st[:, 2:3], st[:, 0:1], -1.0 / D, None, ALU.mult)
        q["vector"].op("tensor_scalar", st[:, 3:4], st[:, 1:2], 1.0 / D, None, ALU.mult)
        q["vector"].op("tensor_tensor", st[:, 4:5], st[:, 2:3], st[:, 2:3], op=ALU.mult)
        q["vector"].guard(ctx.sem("ssv"))
        q["vector"].op("tensor_tensor", st[:, 5:6], st[:, 3:4], st[:, 4:5], op=ALU.subtract,
                       inc=s_dv.bump())
        q["scalar"].wait(s_dv, 2 * i + 1)
        q["scalar"].op("activation", st[:, 6:7], st[:, 5:6], AF.Sqrt, bias=pre["eps_c"],
                       scale=1.0, inc=s_ac.bump())
        q["vector"].wait(s_ac, i + 1)
        q["vector"].op("reciprocal", st[:, 7:8], st[:, 6:7])
        q["vector"].guard(ctx.sem("ssv"))
        if len(bufs["xnbf"]) == 1:
            if i >= 1:
                q["vector"].wait(s_tr, 4 * i)
        elif i >= 2:
            q["vector"].wait(s_tr, 4 * (i - 1))
        q["vector"].op("tensor_scalar", xn, xt, st[:, 2:3], st[:, 7:8], ALU.add, ALU.mult,
                       inc=s_dv.bump())
        q["tensor"].wait(s_dv, 2 * i + 2)
        for grp in range(4):
            g = i * 4 + grp
            pbank = bank_bf16(ctx, 6 + g % 2)
            if g >= 2:
                q["tensor"].wait(s_tr, g - 1)
            for c8 in range(8):
                c = grp * 8 + c8
                q["tensor"].op("transpose", pbank[:, c8 * 128:(c8 + 1) * 128],
                               xn[:, c * 128:(c + 1) * 128], ident,
                               inc=(s_pe.bump() if c8 == 7 else None))
            q["vector"].wait(s_pe, g + 1)
            out_ap = xn_fm.rearrange("p (c j) -> p c j", j=1024)[
                :, grp * 8:(grp + 1) * 8, tb * 128:tb * 128 + 128]
            in_ap = pbank.rearrange("p (c j) -> p c j", j=128)
            q["vector"].op("tensor_copy", out_ap, in_ap, inc=s_tr.bump())
    return s_tr


def build_ffn0(ctx, pre, x_in, x_out, w_T, bias_r, acat_T, bcat, lsi, bval, iota3,
               u_dram, lnbufs, tag):
    q = ctx.q
    nc = ctx.nc
    xn_fm = ctx.named["xn_fm"]

    s_wl = ctx.sem("wl_" + tag)
    acat_bf = ctx.sbuf("acat_" + tag, 32 * 48, BF16)
    bcat_bf = ctx.sbuf("bcat_" + tag, 4096, BF16)
    bias_bf = ctx.sbuf("bias_" + tag, 4096, BF16, partitions=1)
    lsi_i = ctx.sbuf("lsi_" + tag, 4, I32, partitions=1)
    bval_f = ctx.sbuf("bval_" + tag, 1, F32, partitions=1)
    iota3_f = ctx.sbuf("iota3_" + tag, 3, F32, partitions=1)
    u_sc = ctx.sbuf("usc_" + tag, 8, F32, partitions=1)
    uexp = ctx.sbuf("uexp_" + tag, 48, F32, partitions=1)
    ubc = ctx.sbuf("ubc_" + tag, 48, F32)
    hz = ctx.sbuf("hz_" + tag, 48, BF16)
    hzT = ctx.sbuf("hzT_" + tag, TOK, BF16)[0:48, :]
    wbuf = [ctx.sbuf(f"w{i}_" + tag, 32 * 512, BF16) for i in range(2)]
    xold = [ctx.sbuf(f"xo{i}_" + tag, 512, F32) for i in range(2)]
    xnew = [ctx.sbuf(f"xn{i}_" + tag, 512, F32) for i in range(2)]
    lnbufs = dict(lnbufs)
    lnbufs["xsq"] = wbuf[1].bitcast(F32)[:, 0:4096]

    q["gpsimd"].op("dma_start", out=acat_bf.rearrange("p (c k) -> p c k", k=48),
                   in_=acat_T.rearrange("(c p) k -> p c k", c=32), inc=s_wl.bump(16))
    q["gpsimd"].op("dma_start", out=bcat_bf[0:48, :], in_=bcat, inc=s_wl.bump(16))
    q["gpsimd"].op("dma_start", out=bias_bf[0:1, :], in_=bias_r, inc=s_wl.bump(16))
    q["gpsimd"].op("dma_start", out=lsi_i[0:1, :], in_=lsi, inc=s_wl.bump(16))
    q["gpsimd"].op("dma_start", out=bval_f[0:1, :], in_=bval, inc=s_wl.bump(16))
    q["gpsimd"].op("dma_start", out=iota3_f[0:1, :], in_=iota3, inc=s_wl.bump(16))

    s_u = ctx.sem("u_" + tag)
    v = q["vector"]
    ssv = ctx.sem("ssv")
    v.wait(s_wl, 16 * 6)
    v.op("tensor_copy", u_sc[0:1, 0:4], lsi_i[0:1, :])
    v.guard(ssv)
    v.op("tensor_scalar", u_sc[0:1, 0:4], u_sc[0:1, 0:4], bval_f[0:1, 0:1], None, ALU.is_le)
    v.guard(ssv)
    v.op("reduce_sum", u_sc[0:1, 4:5], u_sc[0:1, 0:4], axis=AX.X)
    v.guard(ssv)
    v.op("tensor_scalar", u_sc[0:1, 5:6], u_sc[0:1, 4:5], -2.0, None, ALU.add)
    v.guard(ssv)
    v.op("tensor_scalar", u_sc[0:1, 5:6], u_sc[0:1, 5:6], 0.0, 2.0, ALU.max, ALU.min)
    v.guard(ssv)
    v.op("tensor_scalar", u_sc[0:1, 6:7], u_sc[0:1, 4:5], 2.0, None, ALU.is_ge)
    v.guard(ssv)
    v.op("tensor_scalar", u_sc[0:1, 0:3], iota3_f[0:1, :], u_sc[0:1, 5:6], None, ALU.is_equal)
    v.guard(ssv)
    v.op("tensor_scalar", u_sc[0:1, 0:3], u_sc[0:1, 0:3], u_sc[0:1, 6:7], None, ALU.mult)
    v.guard(ssv)
    for k in range(3):
        v.op("tensor_scalar", uexp[0:1, k * 16:(k + 1) * 16], pre["ones16"],
             u_sc[0:1, k:k + 1], None, ALU.mult, inc=(s_u.bump() if k == 2 else None))
    q["scalar"].wait(s_u, 1)
    q["scalar"].op("dma_start", out=u_dram, in_=uexp[0:1, :], inc=s_u.bump(16))
    q["scalar"].wait(s_u, 17)
    q["scalar"].op("dma_start", out=ubc, in_=u_dram.partition_broadcast(128),
                   inc=s_u.bump(16))

    s_tr = build_ln_xnfm(ctx, pre, x_in, xn_fm, lnbufs, tag)
    tr_total = s_tr.n

    s_h = ctx.sem("h_" + tag)
    s_hz = ctx.sem("hz_" + tag)
    t = q["tensor"]
    t.wait(s_tr, tr_total)
    t.wait(s_wl, 16 * 6)
    for tb in range(TOKB):
        for c in range(DCH):
            t.op("matmul", ctx.psum[5][:, 0:48],
                 xn_fm[:, c * 1024 + tb * 128: c * 1024 + tb * 128 + 128],
                 acat_bf[:, c * 48:(c + 1) * 48],
                 start=(c == 0), stop=(c == DCH - 1),
                 inc=(s_h.bump() if c == DCH - 1 else None))
        q["vector"].wait(s_h, 2 * tb + 1)
        q["vector"].wait(s_u, 33)
        q["vector"].op("tensor_tensor", hz, ctx.psum[5][:, 0:48], ubc, op=ALU.mult,
                       inc=s_hz.bump())
        t.wait(s_hz, 2 * tb + 1)
        t.op("transpose", bank_bf16(ctx, 4)[:, 0:128][0:48, :], hz, pre["ident"],
             inc=s_h.bump())
        q["vector"].wait(s_h, 2 * tb + 2)
        q["vector"].op("tensor_copy", hzT[0:48, tb * 128:(tb + 1) * 128],
                       bank_bf16(ctx, 4)[0:48, 0:128], inc=s_hz.bump())
        if tb < TOKB - 1:
            t.wait(s_hz, 2 * tb + 2)
    hz_total = s_hz.n

    s_mm = ctx.sem("mm_" + tag)
    s_ep = ctx.sem("ep_" + tag)
    s_xl = ctx.sem("xl_" + tag)
    s_st = ctx.sem("st_" + tag)
    n_ep = 0
    for og in range(8):
        wb = wbuf[og % 2]
        if og == 1:
            q["gpsimd"].wait(s_tr, tr_total)
        if og >= 2:
            q["gpsimd"].wait(s_mm, (og - 2) * TOKB + TOKB)
        q["gpsimd"].op("dma_start", out=wb.rearrange("p (c o) -> p c o", o=512),
                       in_=w_T.rearrange("(c p) o -> p c o", c=32)[:, :, og * 512:(og + 1) * 512],
                       inc=s_wl.bump(16))
        wl_val = s_wl.n
        for tb in range(TOKB):
            it = og * TOKB + tb
            bk = it % 4
            par = it % 2
            if it >= 2:
                q["sync"].wait(s_ep, it - 1)
            q["sync"].op("dma_start", out=xold[par],
                         in_=x_in[tb * 128:(tb + 1) * 128, og * 512:(og + 1) * 512],
                         inc=s_xl.bump(16))
            t = q["tensor"]
            t.wait(s_wl, wl_val)
            if tb == 0 and og == 0:
                t.wait(s_hz, hz_total)
                t.wait(s_u, 33)
            if it >= 4:
                t.wait(s_ep, it - 3)
            for c in range(DCH):
                t.op("matmul", ctx.psum[bk],
                     xn_fm[:, c * 1024 + tb * 128: c * 1024 + tb * 128 + 128],
                     wb[:, c * 512:(c + 1) * 512],
                     start=(c == 0), stop=False)
            t.op("matmul", ctx.psum[bk], pre["ones_r"], bias_bf[0:1, og * 512:(og + 1) * 512],
                 start=False, stop=False)
            t.op("matmul", ctx.psum[bk], hzT[0:48, tb * 128:(tb + 1) * 128],
                 bcat_bf[0:48, og * 512:(og + 1) * 512],
                 start=False, stop=True, inc=s_mm.bump())
            q["vector"].wait(s_mm, it + 1)
            q["vector"].wait(s_xl, 16 * (it + 1))
            if it >= 2:
                q["vector"].wait(s_st, 16 * (it - 1))
            q["vector"].op("scalar_tensor_tensor", xnew[par], ctx.psum[bk], 0.5, xold[par],
                           op0=ALU.mult, op1=ALU.add, inc=s_ep.bump())
            n_ep += 1
            q["scalar"].wait(s_ep, n_ep)
            q["scalar"].op("dma_start", out=x_out[tb * 128:(tb + 1) * 128, og * 512:(og + 1) * 512],
                           in_=xnew[par], inc=s_st.bump(16))
    return s_st


def build_ffn1(ctx, pre, x_in, x_out, w_T, bias_r, a_T, bflat_T, router_T,
               lnbufs, tag):
    q = ctx.q
    xn_fm = ctx.named["xn_fm"]
    ssv = ctx.sem("ssv")

    s_wl = ctx.sem("wl_" + tag)
    a_bf = ctx.sbuf("a_" + tag, 32 * 64, BF16)
    rt_bf = ctx.sbuf("rt_" + tag, 32 * 16, BF16)
    bias_bf = ctx.sbuf("bias_" + tag, 4096, BF16, partitions=1)
    cbuf = ctx.sbuf("c_" + tag, 16, F32)
    csc = ctx.sbuf("csc_" + tag, 8, F32)
    cm1 = ctx.sbuf("cm1_" + tag, 16, F32)
    pmb = ctx.sbuf("pm_" + tag, 16, F32)
    elog = ctx.sbuf("el_" + tag, 16, F32)
    z_tm = ctx.sbuf("ztm_" + tag, 1024, BF16)
    z_fm = ctx.sbuf("zfm_" + tag, 8 * 1024, BF16)
    wbuf = [ctx.sbuf(f"w{i}_" + tag, 40 * 512, BF16) for i in range(2)]
    a_f32 = a_bf.bitcast(F32)
    xold = [a_f32[:, 0:512], a_f32[:, 512:1024]]
    xnb_f32 = lnbufs["xnbf"][0].bitcast(F32)
    xnew = [xnb_f32[:, 0:512], xnb_f32[:, 512:1024]]
    lnbufs = dict(lnbufs)
    lnbufs["xsq"] = wbuf[1].bitcast(F32)[:, 0:4096]

    q["gpsimd"].op("dma_start", out=a_bf.rearrange("p (c k) -> p c k", k=64),
                   in_=a_T.rearrange("(c p) k -> p c k", c=32), inc=s_wl.bump(16))
    q["gpsimd"].op("dma_start", out=rt_bf.rearrange("p (c k) -> p c k", k=16),
                   in_=router_T.rearrange("(c p) k -> p c k", c=32), inc=s_wl.bump(16))
    q["gpsimd"].op("dma_start", out=bias_bf[0:1, :], in_=bias_r, inc=s_wl.bump(16))

    s_tr = build_ln_xnfm(ctx, pre, x_in, xn_fm, lnbufs, tag)
    tr_total = s_tr.n

    s_rt = ctx.sem("rt_" + tag)
    s_zc = ctx.sem("zc_" + tag)
    s_za = ctx.sem("za_" + tag)
    t = q["tensor"]
    v = q["vector"]
    t.wait(s_tr, tr_total)
    t.wait(s_wl, 48)
    for tb in range(TOKB):
        if tb >= 1:
            t.wait(s_zc, 10 * (tb - 1) + 2)
            t.wait(s_za, tb)
        for c in range(DCH):
            t.op("matmul", ctx.psum[5][:, 0:16],
                 xn_fm[:, c * 1024 + tb * 128: c * 1024 + tb * 128 + 128],
                 rt_bf[:, c * 16:(c + 1) * 16],
                 start=(c == 0), stop=(c == DCH - 1),
                 inc=(s_rt.bump() if c == DCH - 1 else None))
        for c in range(DCH):
            t.op("matmul", ctx.psum[4][:, 0:64],
                 xn_fm[:, c * 1024 + tb * 128: c * 1024 + tb * 128 + 128],
                 a_bf[:, c * 64:(c + 1) * 64],
                 start=(c == 0), stop=(c == DCH - 1),
                 inc=(s_rt.bump() if c == DCH - 1 else None))
        if tb >= 1:
            v.wait(s_za, tb)
        v.wait(s_rt, 10 * tb + 1)
        v.op("reduce_max", csc[:, 0:1], ctx.psum[5][:, 0:16], axis=AX.X, negate=True)
        v.guard(ssv)
        v.op("engine_nop", inc=s_zc.bump())
        q["scalar"].wait(s_zc, 10 * tb + 1)
        q["scalar"].op("activation", elog[:, 0:16], ctx.psum[5][:, 0:16], AF.Exp,
                       bias=csc[:, 0:1], scale=1.0, accum_out=csc[:, 1:2],
                       inc=s_za.bump())
        v.wait(s_za, tb + 1)
        v.op("reduce_max", csc[:, 2:3], elog[:, 0:16], axis=AX.X)
        v.guard(ssv)
        v.op("tensor_scalar", cm1[:, 0:16], elog[:, 0:16], csc[:, 2:3], None, ALU.is_ge)
        v.guard(ssv)
        v.op("tensor_scalar", cm1[:, 0:16], cm1[:, 0:16], -1.0, 1.0, ALU.mult, ALU.add)
        v.guard(ssv)
        v.op("tensor_tensor", pmb[:, 0:16], elog[:, 0:16], cm1[:, 0:16], op=ALU.mult)
        v.guard(ssv)
        v.op("reduce_max", csc[:, 3:4], pmb[:, 0:16], axis=AX.X)
        v.guard(ssv)
        v.op("tensor_scalar", pmb[:, 0:16], pmb[:, 0:16], csc[:, 3:4], None, ALU.is_ge)
        v.guard(ssv)
        v.op("tensor_tensor", cm1[:, 0:16], pmb[:, 0:16], cm1[:, 0:16], op=ALU.subtract)
        v.guard(ssv)
        v.op("tensor_scalar", cm1[:, 0:16], cm1[:, 0:16], 1.0, None, ALU.add)
        v.guard(ssv)
        v.op("tensor_tensor", csc[:, 4:5], csc[:, 2:3], csc[:, 3:4], op=ALU.add)
        v.guard(ssv)
        v.op("scalar_tensor_tensor", csc[:, 4:5], csc[:, 1:2], 1e-6, csc[:, 4:5],
             op0=ALU.mult, op1=ALU.add)
        v.guard(ssv)
        v.op("reciprocal", csc[:, 5:6], csc[:, 4:5])
        v.guard(ssv)
        v.op("tensor_tensor", cbuf[:, 0:16], elog[:, 0:16], cm1[:, 0:16], op=ALU.mult)
        v.guard(ssv)
        v.op("tensor_scalar", cbuf[:, 0:16], cbuf[:, 0:16], csc[:, 5:6], 0.5, ALU.mult, ALU.mult)
        v.guard(ssv)
        v.wait(s_rt, 10 * tb + 2)
        if tb >= 1:
            v.wait(s_rt, 10 * tb)
        for e in range(16):
            v.op("tensor_scalar", z_tm[:, e * 64:(e + 1) * 64], ctx.psum[4][:, 0:64],
                 cbuf[:, e:e + 1], None, ALU.mult,
                 inc=(s_zc.bump() if e == 15 else None))
        t.wait(s_zc, 10 * tb + 2)
        for zc in range(8):
            pbank = bank_bf16(ctx, 6 + zc % 2)
            if zc >= 2:
                t.wait(s_zc, 10 * tb + zc + 1)
            t.op("transpose", pbank[:, 0:128], z_tm[:, zc * 128:(zc + 1) * 128],
                 pre["ident"], inc=s_rt.bump())
            v.wait(s_rt, 10 * tb + 3 + zc)
            v.op("tensor_copy", z_fm[:, zc * 1024 + tb * 128: zc * 1024 + tb * 128 + 128],
                 pbank[:, 0:128], inc=s_zc.bump())
    zc_total = s_zc.n

    s_mm = ctx.sem("mm_" + tag)
    s_ep = ctx.sem("ep_" + tag)
    s_xl = ctx.sem("xl_" + tag)
    s_st = ctx.sem("st_" + tag)
    n_ep = 0
    for og in range(8):
        wb = wbuf[og % 2]
        if og == 1:
            q["gpsimd"].wait(s_tr, tr_total)
        if og >= 2:
            q["gpsimd"].wait(s_mm, (og - 2) * TOKB + TOKB)
        q["gpsimd"].op("dma_start", out=wb[:, 0:32 * 512].rearrange("p (c o) -> p c o", o=512),
                       in_=w_T.rearrange("(c p) o -> p c o", c=32)[:, :, og * 512:(og + 1) * 512],
                       inc=s_wl.bump(16))
        q["gpsimd"].op("dma_start", out=wb[:, 32 * 512:40 * 512].rearrange("p (c o) -> p c o", o=512),
                       in_=bflat_T.rearrange("(c p) o -> p c o", c=8)[:, :, og * 512:(og + 1) * 512],
                       inc=s_wl.bump(16))
        wl_val = s_wl.n
        for tb in range(TOKB):
            it = og * TOKB + tb
            bk = it % 4
            par = it % 2
            if it < 2:
                q["sync"].wait(s_zc, zc_total)
            if it >= 2:
                q["sync"].wait(s_ep, it - 1)
            q["sync"].op("dma_start", out=xold[par],
                         in_=x_in[tb * 128:(tb + 1) * 128, og * 512:(og + 1) * 512],
                         inc=s_xl.bump(16))
            t = q["tensor"]
            t.wait(s_wl, wl_val)
            if tb == 0 and og == 0:
                t.wait(s_zc, zc_total)
            if it >= 4:
                t.wait(s_ep, it - 3)
            for c in range(DCH):
                t.op("matmul", ctx.psum[bk],
                     xn_fm[:, c * 1024 + tb * 128: c * 1024 + tb * 128 + 128],
                     wb[:, c * 512:(c + 1) * 512],
                     start=(c == 0), stop=False)
            t.op("matmul", ctx.psum[bk], pre["ones_r"], bias_bf[0:1, og * 512:(og + 1) * 512],
                 start=False, stop=False)
            for zc in range(8):
                t.op("matmul", ctx.psum[bk],
                     z_fm[:, zc * 1024 + tb * 128: zc * 1024 + tb * 128 + 128],
                     wb[:, (32 + zc) * 512:(33 + zc) * 512],
                     start=False, stop=(zc == 7), inc=(s_mm.bump() if zc == 7 else None))
            q["vector"].wait(s_mm, it + 1)
            q["vector"].wait(s_xl, 16 * (it + 1))
            if it >= 2:
                q["vector"].wait(s_st, 16 * (it - 1))
            q["vector"].op("scalar_tensor_tensor", xnew[par], ctx.psum[bk], 0.5, xold[par],
                           op0=ALU.mult, op1=ALU.add, inc=s_ep.bump())
            n_ep += 1
            q["scalar"].wait(s_ep, n_ep)
            q["scalar"].op("dma_start", out=x_out[tb * 128:(tb + 1) * 128, og * 512:(og + 1) * 512],
                           in_=xnew[par], inc=s_st.bump(16))
    return s_st


QSCALE1 = 0.08838834764831845


def build_attn1(ctx, pre, x_in, x_out, wqkv, wo, strip_h, sh_ex, b_in, b_out,
                oT_dram, lnbufs, tag):
    q = ctx.q
    nc = ctx.nc
    xn_fm = ctx.named["xn_fm"]
    ssv = ctx.sem("ssv")
    NKV, G = 8, 4

    exv = ctx.sbuf("exv_" + tag, 16384, BF16)
    exv_off = ctx.sb_off - 32768
    staged = ctx.at(exv_off, 16 * 512, F32)
    kT = ctx.sbuf("kT_" + tag, 2048, BF16)
    kT_off = ctx.sb_off - 4096
    vTf = ctx.sbuf("vTf_" + tag, 2048, BF16)
    vtok = ctx.sbuf("vtok_" + tag, 2048, BF16)
    qT = ctx.sbuf("qT_" + tag, 1024, BF16)
    pT = ctx.sbuf("pT_" + tag, 16 * 512, BF16)
    pT_off = ctx.sb_off - 16384
    stripb = [ctx.sbuf(f"strip{i}_" + tag, 2944, F32) for i in range(2)]
    wkv = ctx.sbuf("wkv_" + tag, 32 * 256, BF16)
    stats = ctx.sbuf("stats_" + tag, 32, F32)
    negMc = ctx.sbuf("negM_" + tag, 1, F32)
    srow = ctx.sbuf("srow_" + tag, 512, F32, partitions=1)
    rdenb = ctx.sbuf("rden_" + tag, 512, F32)
    oTt = ctx.sbuf("oTt_" + tag, 512, BF16)
    oT_full = ctx.at(0, 0, BF16)
    xnfm_off = ctx.named["xn_fm_off"]
    oT_full = ctx.at(xnfm_off, 32768, BF16)
    wo_buf = [ctx.at(exv_off, 32 * 512, BF16), ctx.at(pT_off, 32 * 512, BF16)]
    xold = [ctx.at(kT_off, 512, F32), ctx.at(kT_off + 2048, 512, F32)]
    xnew = [ctx.at(kT_off + 4096, 512, F32), ctx.at(kT_off + 6144, 512, F32)]
    lnbufs = dict(lnbufs)
    lnbufs["xsq"] = ctx.at(exv_off, 4096, F32)

    s_tr = build_ln_xnfm(ctx, pre, x_in, xn_fm, lnbufs, tag)
    tr_total = s_tr.n

    s_wl = ctx.sem("wl_" + tag)
    s_p2 = ctx.sem("p2_" + tag)
    s_e2 = ctx.sem("e2_" + tag)
    s_ex = ctx.sem("ex_" + tag)
    t = q["tensor"]
    v = q["vector"]
    t.wait(s_tr, tr_total)
    for h in range(NKV):
        if h >= 1:
            q["gpsimd"].wait(s_p2, 2 * h)
        q["gpsimd"].op("dma_start",
                       out=wkv.rearrange("p (c o) -> p c o", o=256)[:, :, 0:128],
                       in_=wqkv.rearrange("(c p) o -> p c o", c=32)[:, :, 4096 + h * 128:4224 + h * 128],
                       inc=s_wl.bump(16))
        q["gpsimd"].op("dma_start",
                       out=wkv.rearrange("p (c o) -> p c o", o=256)[:, :, 128:256],
                       in_=wqkv.rearrange("(c p) o -> p c o", c=32)[:, :, 5120 + h * 128:5248 + h * 128],
                       inc=s_wl.bump(16))
        wl_now = s_wl.n
        t.wait(s_wl, wl_now)
        if h >= 2:
            t.wait(s_e2, 4 * (h - 1))
        for c in range(DCH):
            lk = wkv[:, c * 256:c * 256 + 128]
            lv = wkv[:, c * 256 + 128:c * 256 + 256]
            xa = xn_fm[:, c * 1024:c * 1024 + 512]
            xb = xn_fm[:, c * 1024 + 512:c * 1024 + 1024]
            st0, sp0 = (c == 0), (c == DCH - 1)
            t.op("matmul", ctx.psum[0], lk, xa, start=st0, stop=sp0)
            t.op("matmul", ctx.psum[1], lk, xb, start=st0, stop=sp0)
            t.op("matmul", ctx.psum[2], lv, xa, start=st0, stop=sp0)
            t.op("matmul", ctx.psum[3], lv, xb, start=st0, stop=sp0,
                 inc=(s_p2.bump(2) if sp0 else None))
        v.wait(s_p2, 2 * (h + 1))
        v.op("tensor_copy", exv[:, h * 1024:h * 1024 + 512], ctx.psum[0][:, 0:512], inc=s_e2.bump())
        v.op("tensor_copy", exv[:, h * 1024 + 512:h * 1024 + 1024], ctx.psum[1][:, 0:512], inc=s_e2.bump())
        v.op("tensor_copy", exv[:, 8192 + h * 1024:8192 + h * 1024 + 512], ctx.psum[2][:, 0:512], inc=s_e2.bump())
        v.op("tensor_copy", exv[:, 8192 + h * 1024 + 512:8192 + h * 1024 + 1024], ctx.psum[3][:, 0:512], inc=s_e2.bump())
    e2_total = s_e2.n
    q["sync"].wait(s_e2, e2_total)

    def _exdma(eng, core):
        par = core % 2
        eng.dma_start(out=sh_ex[:, par * 16384:(par + 1) * 16384], in_=exv[:]).then_inc(
            ctx.sems["ex_" + tag].h, 16)

    q["sync"].switch_core(_exdma)
    s_ex.n += 16
    q["gpsimd"].wait(s_ex, 16)
    q["gpsimd"].op("collective_compute", "AllReduce", ALU.add,
                   replica_groups=[[0, 1], [2, 3], [4, 5], [6, 7]],
                   ins=[b_in], outs=[b_out], inc=s_ex.bump(1))
    ex_done = s_ex.n

    s_as = ctx.sem("as_" + tag)
    s_q = ctx.sem("q_" + tag)
    s_qe = ctx.sem("qe_" + tag)
    s_sc = ctx.sem("sc_" + tag)
    s_sg = ctx.sem("sg_" + tag)
    s_xp = ctx.sem("xp_" + tag)
    s_av = ctx.sem("av_" + tag)
    s_on = ctx.sem("on_" + tag)
    s_os = ctx.sem("os_" + tag)
    for kvh in range(NKV):
        if kvh == 0:
            q["scalar"].wait(s_ex, ex_done)
        else:
            q["scalar"].wait(s_sc, ctx.sems["sc_" + tag].n)
            q["scalar"].wait(s_q, ctx.sems["q_" + tag].n)
        for par in range(2):
            q["scalar"].op("dma_start",
                           out=kT.rearrange("p (m j) -> p m j", j=256)[:, :, par * 128:(par + 1) * 128],
                           in_=sh_ex[:, par * 16384 + kvh * 1024: par * 16384 + (kvh + 1) * 1024]
                           .rearrange("p (m j) -> p m j", j=128),
                           inc=s_as.bump(16))
            q["scalar"].op("dma_start",
                           out=vTf.rearrange("p (m j) -> p m j", j=256)[:, :, par * 128:(par + 1) * 128],
                           in_=sh_ex[:, par * 16384 + 8192 + kvh * 1024: par * 16384 + 8192 + (kvh + 1) * 1024]
                           .rearrange("p (m j) -> p m j", j=128),
                           inc=s_as.bump(16))
        as_now = s_as.n
        t.wait(s_as, as_now)
        if kvh >= 1:
            t.wait(s_av, ctx.sems["av_" + tag].n)
        for kb in range(16):
            pb = bank_bf16(ctx, 6 + kb % 2)
            if kb >= 2:
                t.wait(s_qe, ctx.sems["qe_" + tag].n - 1)
            t.op("transpose", pb[:, 0:128], vTf[:, kb * 128:(kb + 1) * 128], pre["ident"],
                 inc=s_q.bump())
            v.wait(s_q, ctx.sems["q_" + tag].n)
            v.op("tensor_copy", vtok[:, kb * 128:(kb + 1) * 128], pb[:, 0:128],
                 inc=s_qe.bump())
        for g in range(G):
            h = kvh * G + g
            if kvh == 0 and g == 0:
                q["gpsimd"].wait(s_p2, 16)
            q["gpsimd"].wait(s_q, ctx.sems["q_" + tag].n)
            q["gpsimd"].op("dma_start",
                           out=wkv.rearrange("p (c o) -> p c o", o=256)[:, :, 0:128],
                           in_=wqkv.rearrange("(c p) o -> p c o", c=32)[:, :, h * 128:(h + 1) * 128],
                           inc=s_wl.bump(16))
            wl_now = s_wl.n
            q["gpsimd"].wait(s_sg, ctx.sems["sg_" + tag].n)
            q["gpsimd"].op("dma_start", out=stripb[h % 2],
                           in_=strip_h[h * 128:(h + 1) * 128, :], inc=s_wl.bump(16))
            wl_strip = s_wl.n
            t.wait(s_wl, wl_now)
            if h >= 1:
                t.wait(s_sc, ctx.sems["sc_" + tag].n)
            for c in range(DCH):
                lq = wkv[:, c * 256:c * 256 + 128]
                st0, sp0 = (c == 0), (c == DCH - 1)
                t.op("matmul", ctx.psum[4], lq, xn_fm[:, c * 1024:c * 1024 + 512], start=st0, stop=sp0)
                t.op("matmul", ctx.psum[5], lq, xn_fm[:, c * 1024 + 512:c * 1024 + 1024], start=st0, stop=sp0,
                     inc=(s_q.bump() if sp0 else None))
            v.wait(s_q, ctx.sems["q_" + tag].n)
            v.op("tensor_scalar", qT[:, 0:512], ctx.psum[4], QSCALE1, None, ALU.mult)
            v.op("tensor_scalar", qT[:, 512:1024], ctx.psum[5], QSCALE1, None, ALU.mult,
                 inc=s_qe.bump())
            qe_q = s_qe.n
            for half in range(2):
                ncnt = 8 if half == 0 else 16
                s0 = half * 4
                v.wait(s_xp, ctx.sems["xp_" + tag].n)
                sg0 = s_sg.n
                for kb in range(ncnt):
                    sb = kb % 2
                    t.wait(s_qe, qe_q)
                    if kb >= 2:
                        t.wait(s_sg, sg0 + 2 * kb - 3)
                    t.op("matmul", ctx.psum[sb], kT[:, kb * 128:(kb + 1) * 128],
                         qT[:, half * 512:(half + 1) * 512], start=True, stop=True,
                         inc=s_sc.bump())
                    v.wait(s_sc, ctx.sems["sc_" + tag].n)
                    v.wait(s_wl, wl_strip)
                    base = 896 + 256 * s0 - 128 * kb
                    strip_ap = stripb[h % 2][:, base:base + 1024] \
                        .rearrange("p (m j) -> p m j", j=256)[:, :, 0:128]
                    v.op("scalar_tensor_tensor",
                         staged[:, kb * 512:(kb + 1) * 512].rearrange("p (m j) -> p m j", j=128),
                         ctx.psum[sb].rearrange("p (m j) -> p m j", j=128),
                         1.0, strip_ap, op0=ALU.mult, op1=ALU.add, inc=s_sg.bump())
                    v.op("reduce_max", stats[:, kb:kb + 1],
                         staged[:, kb * 512:(kb + 1) * 512], axis=AX.X, inc=s_sg.bump())
                v.wait(s_sg, s_sg.n)
                v.op("reduce_max", stats[:, 16:17], stats[:, 0:ncnt], axis=AX.X)
                v.guard(ssv)
                v.op("engine_nop", inc=s_sg.bump())
                t.wait(s_sg, ctx.sems["sg_" + tag].n)
                t.op("transpose", ctx.psum[3][0:1, 0:128], stats[:, 16:17], pre["identf"],
                     inc=s_av.bump())
                v.wait(s_av, ctx.sems["av_" + tag].n)
                v.op("reduce_max", negMc[0:1, 0:1], ctx.psum[3][0:1, 0:128], axis=AX.X,
                     negate=True)
                v.guard(ssv)
                v.op("engine_nop", inc=s_on.bump())
                t.wait(s_on, ctx.sems["on_" + tag].n)
                t.op("matmul", ctx.psum[3][:, 0:1], pre["onesf_r"], negMc[0:1, 0:1],
                     start=True, stop=True, inc=s_av.bump())
                v.wait(s_av, ctx.sems["av_" + tag].n)
                v.op("tensor_copy", negMc[:, 0:1], ctx.psum[3][:, 0:1])
                v.guard(ssv)
                v.op("engine_nop", inc=s_on.bump())
                q["scalar"].wait(s_on, ctx.sems["on_" + tag].n)
                q["scalar"].wait(s_av, ctx.sems["av_" + tag].n)
                for kb in range(ncnt):
                    q["scalar"].op("activation", pT[:, kb * 512:(kb + 1) * 512],
                                   staged[:, kb * 512:(kb + 1) * 512], AF.Exp,
                                   bias=negMc[:, 0:1], scale=1.0, inc=s_xp.bump())
                xp_now = s_xp.n
                t.wait(s_xp, xp_now)
                for kb in range(ncnt):
                    t.op("matmul", ctx.psum[3][0:1, 0:512], pre["ones_c"],
                         pT[:, kb * 512:(kb + 1) * 512], start=(kb == 0), stop=(kb == ncnt - 1))
                for kb in range(ncnt):
                    t.op("matmul", ctx.psum[2], vtok[:, kb * 128:(kb + 1) * 128],
                         pT[:, kb * 512:(kb + 1) * 512], start=(kb == 0), stop=(kb == ncnt - 1),
                         inc=(s_av.bump() if kb == ncnt - 1 else None))
                v.wait(s_av, ctx.sems["av_" + tag].n)
                v.op("reciprocal", srow[0:1, 0:512], ctx.psum[3][0:1, 0:512])
                v.guard(ssv)
                v.op("engine_nop", inc=s_on.bump())
                t.wait(s_on, ctx.sems["on_" + tag].n)
                t.op("matmul", ctx.psum[3], pre["onesf_r"], srow[0:1, 0:512],
                     start=True, stop=True, inc=s_av.bump())
                v.wait(s_av, ctx.sems["av_" + tag].n)
                v.op("tensor_copy", rdenb, ctx.psum[3])
                v.guard(ssv)
                v.wait(s_os, ctx.sems["os_" + tag].n)
                v.op("tensor_tensor", oTt, ctx.psum[2], rdenb, op=ALU.mult,
                     inc=s_on.bump())
                on_now = s_on.n
                q["sync"].wait(s_on, on_now)
                q["sync"].op("dma_start",
                             out=oT_dram[:, h * 1024 + half * 512: h * 1024 + (half + 1) * 512],
                             in_=oTt, inc=s_os.bump(16))
                t.wait(s_on, on_now)
    os_total = s_os.n
    q_total = s_q.n
    xp_total = s_xp.n
    av_total = s_av.n
    sg_total = s_sg.n

    s_ol = ctx.sem("ol_" + tag)
    s_mm = ctx.sem("mm4_" + tag)
    s_ep = ctx.sem("ep4_" + tag)
    s_xl = ctx.sem("xl4_" + tag)
    s_st = ctx.sem("st4_" + tag)
    q["sync"].wait(s_os, os_total)
    q["sync"].wait(s_q, q_total)
    q["sync"].op("dma_start", out=oT_full, in_=oT_dram[:, :], inc=s_ol.bump(16))
    n_ep = 0
    for og in range(8):
        wb = wo_buf[og % 2]
        if og < 2:
            q["gpsimd"].wait(s_xp, xp_total)
            q["gpsimd"].wait(s_av, av_total)
            q["gpsimd"].wait(s_sg, sg_total)
        if og >= 2:
            q["gpsimd"].wait(s_mm, (og - 2) * TOKB + TOKB)
        q["gpsimd"].op("dma_start", out=wb.rearrange("p (c o) -> p c o", o=512),
                       in_=wo.rearrange("(c p) o -> p c o", c=32)[:, :, og * 512:(og + 1) * 512],
                       inc=s_wl.bump(16))
        wl_val = s_wl.n
        for tb in range(TOKB):
            it = og * TOKB + tb
            bk = it % 4
            par = it % 2
            if it < 2:
                q["sync"].wait(s_av, av_total)
                q["sync"].wait(s_xp, xp_total)
            if it >= 2:
                q["sync"].wait(s_ep, it - 1)
            q["sync"].op("dma_start", out=xold[par],
                         in_=x_in[tb * 128:(tb + 1) * 128, og * 512:(og + 1) * 512],
                         inc=s_xl.bump(16))
            t = q["tensor"]
            t.wait(s_wl, wl_val)
            if it == 0:
                t.wait(s_ol, 16)
            if it >= 4:
                t.wait(s_ep, it - 3)
            for c in range(DCH):
                t.op("matmul", ctx.psum[bk],
                     oT_full[:, c * 1024 + tb * 128: c * 1024 + tb * 128 + 128],
                     wb[:, c * 512:(c + 1) * 512],
                     start=(c == 0), stop=(c == DCH - 1),
                     inc=(s_mm.bump() if c == DCH - 1 else None))
            q["vector"].wait(s_mm, it + 1)
            q["vector"].wait(s_xl, 16 * (it + 1))
            if it >= 2:
                q["vector"].wait(s_st, 16 * (it - 1))
            q["vector"].op("scalar_tensor_tensor", xnew[par], ctx.psum[bk], 1.0, xold[par],
                           op0=ALU.mult, op1=ALU.add, inc=s_ep.bump())
            n_ep += 1
            q["scalar"].wait(s_ep, n_ep)
            q["scalar"].op("dma_start", out=x_out[tb * 128:(tb + 1) * 128, og * 512:(og + 1) * 512],
                           in_=xnew[par], inc=s_st.bump(16))
    return s_st

QSCALE0 = 0.07216878364870323


def build_attn0(ctx, pre, x_in, x_out, wqaT, qnw_r, wqbTp, wkvaT, kvnw_r, wkvbT,
                woT, strip_c, cs_own, cs_fm, sh_kv, b_in, b_out, oT_dram,
                lnbufs, tag):
    import os as _os
    A0PH = int(_os.environ.get("A0PH", "9"))
    q = ctx.q
    xn_fm = ctx.named["xn_fm"]
    ssv = ctx.sem("ssv")

    exv = ctx.sbuf("exv_" + tag, 8 * 576, BF16)
    exv_off = ctx.named["exv_" + tag + "_off"]
    kvn_tm = ctx.sbuf("kvtm_" + tag, 16 * 576, BF16)
    staged = ctx.at(exv_off, 16 * 256, F32)
    kvc_fm = ctx.sbuf("kvc_" + tag, 4 * 2048, BF16)
    kvc_off = ctx.named["kvc_" + tag + "_off"]
    kpe_fm = ctx.sbuf("kpe_" + tag, 2048, BF16)
    kTh = ctx.sbuf("kTh_" + tag, 2048, BF16)
    vtok = ctx.sbuf("vtok_" + tag, 2048, BF16)
    vtok_off = ctx.named["vtok_" + tag + "_off"]
    pT = ctx.sbuf("pT_" + tag, 16 * 256, BF16)
    pT_off = ctx.named["pT_" + tag + "_off"]
    wkvb = [ctx.sbuf(f"wkvb{i}_" + tag, 4 * 256, BF16) for i in range(2)]
    ql_fm = ctx.sbuf("ql_" + tag, 12 * 1024, BF16)
    ql_off = ctx.named["ql_" + tag + "_off"]
    qTh = ctx.sbuf("qTh_" + tag, 1024, BF16)
    qpe = ctx.sbuf("qpe_" + tag, 1024, F32)
    qpe2 = ctx.sbuf("qpe2_" + tag, 1024, BF16)
    wqb = [ctx.sbuf(f"wqb{i}_" + tag, 12 * 192, BF16) for i in range(2)]

    sqg = ctx.sbuf("sqg_" + tag, 48, F32)
    stats = ctx.sbuf("stats_" + tag, 32, F32)
    negMc = ctx.sbuf("negM_" + tag, 1, F32)
    srow = ctx.sbuf("srow_" + tag, 256, F32, partitions=1)
    rdenb = ctx.sbuf("rden_" + tag, 256, F32)
    oTt = ctx.sbuf("oTt_" + tag, 256, BF16)
    rsc = ctx.sbuf("rsc_" + tag, 256, F32)
    xt_off = ctx.named["xt0_off"]
    stripb = ctx.at(xt_off, 2944, F32)
    csov = ctx.at(ctx.named["xt0_off"] + 11776, 8 * 64, F32)
    sqscr = ctx.at(ctx.named["xt0_off"] + 13824, 512, F32)
    xnb_off = ctx.named["xnb0_off"]
    kvwbc = ctx.at(xnb_off, 512, F32)
    qln = [ctx.at(xnb_off + 2048, 1536, BF16), ctx.at(xnb_off + 5120, 1536, BF16)]
    csfm = ctx.at(xnb_off, 2048, F32)
    wkva_buf = ctx.at(kvc_off, 32 * 576, BF16)
    wqa_buf = ctx.at(kvc_off, 32 * 256, BF16)
    ql_raw = ctx.at(kvc_off + 16384, 8 * 1536, BF16)
    qtmp = ctx.at(exv_off, 1536, F32)
    rps = ctx.at(exv_off + 16384, 2048, F32)
    xnfm_off = ctx.named["xn_fm_off"]
    oT_full = ctx.at(xnfm_off, 32768, BF16)
    wo_buf = [ctx.at(kvc_off, 32 * 512, BF16), ctx.at(ql_off, 32 * 512, BF16)]
    xold = [ctx.at(exv_off, 512, F32), ctx.at(exv_off + 4096, 512, F32)]
    xnew = [ctx.at(exv_off + 2048, 512, F32), ctx.at(exv_off + 6144, 512, F32)]
    lnbufs = dict(lnbufs)
    lnbufs["xsq"] = ctx.at(exv_off, 4096, F32)

    qwbc = ctx.at(exv_off + 6144, 1536, F32)

    s_tr = build_ln_xnfm(ctx, pre, x_in, xn_fm, lnbufs, tag)
    tr_total = s_tr.n

    t = q["tensor"]
    v = q["vector"]
    sc = q["scalar"]
    g_ = q["gpsimd"]
    s_wl = ctx.sem("wl_" + tag)
    s_p2 = ctx.sem("p2_" + tag)
    s_e2 = ctx.sem("e2_" + tag)
    s_a2 = ctx.sem("a2_" + tag)
    s_ex = ctx.sem("ex_" + tag)

    g_.wait(s_tr, tr_total)
    g_.op("dma_start", out=stripb, in_=strip_c, inc=s_wl.bump(16))
    g_.op("dma_start", out=kvwbc, in_=kvnw_r.partition_broadcast(128), inc=s_wl.bump(16))
    g_.op("dma_start", out=csov.rearrange("p (b k) -> p b k", k=64),
          in_=cs_own.rearrange("(b p) k -> p b k", p=128), inc=s_wl.bump(16))
    g_.op("dma_start", out=wkva_buf.rearrange("p (c o) -> p c o", o=576),
          in_=wkvaT.rearrange("(c p) o -> p c o", c=32), inc=s_wl.bump(16))
    wl_small = s_wl.n

    t.wait(s_tr, tr_total)
    t.wait(s_wl, wl_small)
    for tb in range(TOKB):
        if tb >= 1:
            t.wait(s_e2, ctx.sems["e2_" + tag].n)
            t.wait(s_a2, ctx.sems["a2_" + tag].n)
        for c in range(DCH):
            st0, sp0 = (c == 0), (c == DCH - 1)
            xl = xn_fm[:, c * 1024 + tb * 128: c * 1024 + tb * 128 + 128]
            t.op("matmul", ctx.psum[0], xl, wkva_buf[:, c * 576:c * 576 + 512],
                 start=st0, stop=sp0)
            t.op("matmul", ctx.psum[1][:, 0:64], xl, wkva_buf[:, c * 576 + 512:(c + 1) * 576],
                 start=st0, stop=sp0, inc=(s_p2.bump() if sp0 else None))
        sc.wait(s_p2, ctx.sems["p2_" + tag].n)
        sc.op("activation", sqscr, ctx.psum[0], AF.Square, accum_out=stats[:, 0:1])
        sc.guard(ctx.sem("ssa"))
        sc.op("activation", stats[:, 1:2], stats[:, 0:1], AF.Sqrt, bias=pre["eps6_c"],
              scale=1.0 / 512, inc=s_a2.bump())
        v.wait(s_p2, ctx.sems["p2_" + tag].n)
        v.wait(s_a2, ctx.sems["a2_" + tag].n)
        v.op("reciprocal", stats[:, 2:3], stats[:, 1:2])
        v.guard(ssv)
        v.op("scalar_tensor_tensor", exv[:, tb * 576: tb * 576 + 512], ctx.psum[0],
             stats[:, 2:3], kvwbc, op0=ALU.mult, op1=ALU.mult)
        e_ap = ctx.psum[1][:, 0:64:2]
        o_ap = ctx.psum[1][:, 1:64:2]
        cosv = csov[:, tb * 64: tb * 64 + 32]
        sinv = csov[:, tb * 64 + 32: tb * 64 + 64]
        v.op("tensor_tensor", rsc[:, 0:32], e_ap, cosv, op=ALU.mult)
        v.op("tensor_tensor", rsc[:, 32:64], o_ap, sinv, op=ALU.mult)
        v.op("tensor_tensor", rsc[:, 64:96], e_ap, sinv, op=ALU.mult)
        v.op("tensor_tensor", rsc[:, 96:128], o_ap, cosv, op=ALU.mult)
        v.guard(ssv)
        v.op("tensor_tensor", exv[:, tb * 576 + 512: tb * 576 + 544],
             rsc[:, 0:32], rsc[:, 32:64], op=ALU.subtract)
        v.op("tensor_tensor", exv[:, tb * 576 + 544: tb * 576 + 576],
             rsc[:, 64:96], rsc[:, 96:128], op=ALU.add, inc=s_e2.bump())
    e2_kv = s_e2.n
    p2_kv = s_p2.n

    if A0PH < 2:
        return s_e2
    q["sync"].wait(s_e2, e2_kv)

    def _exdma(eng, core):
        par = core % 2
        eng.dma_start(out=sh_kv[:, par * 4608:(par + 1) * 4608], in_=exv[:]).then_inc(
            ctx.sems["ex_" + tag].h, 16)

    q["sync"].switch_core(_exdma)
    s_ex.n += 16
    g_.wait(s_ex, 16)
    g_.op("collective_compute", "AllReduce", ALU.add,
          replica_groups=[[0, 1], [2, 3], [4, 5], [6, 7]],
          ins=[b_in], outs=[b_out], inc=s_ex.bump(1))
    ex_done = s_ex.n
    g_.op("dma_start", out=qwbc, in_=qnw_r.partition_broadcast(128), inc=s_wl.bump(16))
    s_qw = s_wl.n

    if A0PH < 3:
        return s_ex
    a2_kv = s_a2.n
    for og in range(6):
        if og == 0:
            g_.wait(s_p2, p2_kv)
        else:
            g_.wait(s_p2, ctx.sems["p2_" + tag].n)
        g_.op("dma_start", out=wqa_buf.rearrange("p (c o) -> p c o", o=256),
              in_=wqaT.rearrange("(c p) o -> p c o", c=32)[:, :, og * 256:(og + 1) * 256],
              inc=s_wl.bump(16))
        wl_og = s_wl.n
        t.wait(s_wl, wl_og)
        for tb in range(TOKB):
            it = og * TOKB + tb
            bk = it % 2
            if it >= 2:
                t.wait(s_e2, e2_kv + it - 1)
                t.wait(s_a2, a2_kv + it - 1)
            for c in range(DCH):
                t.op("matmul", ctx.psum[bk][:, 0:256],
                     xn_fm[:, c * 1024 + tb * 128: c * 1024 + tb * 128 + 128],
                     wqa_buf[:, c * 256:(c + 1) * 256],
                     start=(c == 0), stop=(c == DCH - 1),
                     inc=(s_p2.bump() if c == DCH - 1 else None))
            sc.wait(s_p2, ctx.sems["p2_" + tag].n)
            sc.op("activation", sqscr[:, 0:256], ctx.psum[bk][:, 0:256], AF.Square,
                  accum_out=sqg[:, tb * 6 + og: tb * 6 + og + 1], inc=s_a2.bump())
            v.wait(s_a2, ctx.sems["a2_" + tag].n)
            v.op("tensor_copy", ql_raw[:, tb * 1536 + og * 256: tb * 1536 + (og + 1) * 256],
                 ctx.psum[bk][:, 0:256], inc=s_e2.bump())
    s_qt = ctx.sem("qt_" + tag)
    for tb in range(TOKB):
        par = tb % 2
        v.wait(s_e2, e2_kv + 48)
        if tb >= 2:
            v.wait(s_qt, 12 * (tb - 1))
        if tb == 0:
            v.wait(s_wl, s_qw)
            v.wait(s_ex, 16)
            v.wait(s_a2, a2_kv + 48)
        else:
            v.wait(s_a2, ctx.sems["a2_" + tag].n)
        v.op("reduce_sum", stats[:, 0:1], sqg[:, tb * 6:(tb + 1) * 6], axis=AX.X)
        v.guard(ssv)
        v.op("engine_nop", inc=s_e2.bump())
        sc.wait(s_e2, ctx.sems["e2_" + tag].n)
        sc.op("activation", stats[:, 1:2], stats[:, 0:1], AF.Sqrt, bias=pre["eps6_c"],
              scale=1.0 / 1536, inc=s_a2.bump())
        v.wait(s_a2, ctx.sems["a2_" + tag].n)
        v.op("reciprocal", stats[:, 2:3], stats[:, 1:2])
        v.guard(ssv)
        v.op("tensor_scalar", qtmp, ql_raw[:, tb * 1536:(tb + 1) * 1536],
             stats[:, 2:3], None, ALU.mult)
        v.guard(ssv)
        v.op("tensor_tensor", qln[par], qtmp, qwbc, op=ALU.mult)
        v.guard(ssv)
        v.op("engine_nop", inc=s_e2.bump())
        t.wait(s_e2, ctx.sems["e2_" + tag].n)
        for cc in range(12):
            pb = bank_bf16(ctx, 6 + cc % 2)
            if tb * 12 + cc >= 2:
                t.wait(s_qt, tb * 12 + cc - 1)
            t.op("transpose", pb[:, 0:128], qln[par][:, cc * 128:(cc + 1) * 128],
                 pre["ident"], inc=s_p2.bump())
            v.wait(s_p2, ctx.sems["p2_" + tag].n)
            v.op("tensor_copy", ql_fm[:, cc * 1024 + tb * 128: cc * 1024 + tb * 128 + 128],
                 pb[:, 0:128], inc=s_qt.bump())
    qt_total = s_qt.n

    if A0PH < 4:
        return s_qt
    g_.wait(s_qt, qt_total)
    g_.op("dma_start", out=csfm[0:64, :], in_=cs_fm, inc=s_wl.bump(16))
    s_csf = s_wl.n

    s_as = ctx.sem("as_" + tag)
    s_kf = ctx.sem("kf_" + tag)
    sc.wait(s_ex, ex_done)
    for par in range(2):
        sc.op("dma_start",
              out=kvn_tm.rearrange("p (m j) -> p m j", j=1152)[:, :, par * 576:(par + 1) * 576],
              in_=sh_kv[:, par * 4608:(par + 1) * 4608].rearrange("p (m j) -> p m j", j=576),
              inc=s_as.bump(16))
    as_total = s_as.n
    t.wait(s_as, as_total)
    t.wait(s_qt, qt_total)
    for kb in range(16):
        for cc in range(4):
            g = kb * 5 + cc
            pb = bank_bf16(ctx, 6 + g % 2)
            if g >= 2:
                t.wait(s_kf, g - 1)
            t.op("transpose", pb[:, 0:128],
                 kvn_tm[:, kb * 576 + cc * 128: kb * 576 + (cc + 1) * 128],
                 pre["ident"], inc=s_p2.bump())
            v.wait(s_p2, ctx.sems["p2_" + tag].n)
            v.op("tensor_copy", kvc_fm[:, cc * 2048 + kb * 128: cc * 2048 + (kb + 1) * 128],
                 pb[:, 0:128], inc=s_kf.bump())
        g = kb * 5 + 4
        pb = bank_bf16(ctx, 6 + g % 2)
        if g >= 2:
            t.wait(s_kf, g - 1)
        t.op("transpose", pb[0:64, 0:128],
             kvn_tm[:, kb * 576 + 512: kb * 576 + 576], pre["ident"], inc=s_p2.bump())
        v.wait(s_p2, ctx.sems["p2_" + tag].n)
        v.op("tensor_copy", kpe_fm[0:64, kb * 128:(kb + 1) * 128], pb[0:64, 0:128],
             inc=s_kf.bump())
    kf_total = s_kf.n

    if A0PH < 5:
        return s_kf
    s_sc = ctx.sem("sc_" + tag)
    s_sg = ctx.sem("sg_" + tag)
    s_xp = ctx.sem("xp_" + tag)
    s_av = ctx.sem("av_" + tag)
    s_on = ctx.sem("on_" + tag)
    s_os = ctx.sem("os_" + tag)
    s_q = ctx.sem("q_" + tag)
    s_qe = ctx.sem("qe_" + tag)
    EXT = [4, 8, 12, 16]
    NHEADS = int(_os.environ.get("A0NH", "32"))
    for h in range(NHEADS):
        if h == 0:
            g_.wait(s_e2, ctx.sems["e2_" + tag].n)
            g_.wait(s_kf, kf_total)
        if h >= 2:
            g_.wait(s_q, ctx.sems["q_" + tag].n)
        g_.op("dma_start", out=wqb[h % 2].rearrange("p (c o) -> p c o", o=192),
              in_=wqbTp.rearrange("(c p) o -> p c o", c=12)[:, :, h * 192:(h + 1) * 192],
              inc=s_wl.bump(16))
        g_.op("dma_start", out=wkvb[h % 2].rearrange("p (c o) -> p c o", o=256),
              in_=wkvbT.rearrange("(c p) o -> p c o", c=4)[:, :, h * 256:(h + 1) * 256],
              inc=s_wl.bump(16))
        wl_h = s_wl.n
        wq = wqb[h % 2]
        wkv_ = wkvb[h % 2]
        t.wait(s_wl, wl_h)
        if h == 0:
            t.wait(s_qt, qt_total)
            t.wait(s_kf, kf_total)
        if h >= 1:
            t.wait(s_sc, ctx.sems["sc_" + tag].n)
            t.wait(s_av, ctx.sems["av_" + tag].n)
            t.wait(s_qe, ctx.sems["qe_" + tag].n)
        for c in range(12):
            st0, sp0 = (c == 0), (c == 11)
            t.op("matmul", ctx.psum[4], wq[:, c * 192:c * 192 + 128],
                 ql_fm[:, c * 1024:c * 1024 + 512], start=st0, stop=sp0)
            t.op("matmul", ctx.psum[5], wq[:, c * 192:c * 192 + 128],
                 ql_fm[:, c * 1024 + 512:c * 1024 + 1024], start=st0, stop=sp0)
            t.op("matmul", ctx.psum[3][0:64, :], wq[:, c * 192 + 128:(c + 1) * 192],
                 ql_fm[:, c * 1024:c * 1024 + 512], start=st0, stop=sp0)
            t.op("matmul", ctx.psum[6][0:64, :], wq[:, c * 192 + 128:(c + 1) * 192],
                 ql_fm[:, c * 1024 + 512:c * 1024 + 1024], start=st0, stop=sp0,
                 inc=(s_q.bump() if sp0 else None))
        v.wait(s_q, ctx.sems["q_" + tag].n)
        v.op("tensor_scalar", qTh[:, 0:512], ctx.psum[4], QSCALE0, None, ALU.mult)
        v.op("tensor_scalar", qTh[:, 512:1024], ctx.psum[5], QSCALE0, None, ALU.mult)
        v.op("tensor_scalar", qpe[0:64, 0:512], ctx.psum[3][0:64, :], QSCALE0, None, ALU.mult)
        v.op("tensor_scalar", qpe[0:64, 512:1024], ctx.psum[6][0:64, :], QSCALE0, None, ALU.mult)
        v.guard(ssv)
        E, O = qpe[0:32, :], qpe[32:64, :]
        if h == 0:
            v.wait(s_wl, s_csf)
        v.op("tensor_tensor", rps[0:32, 0:1024], E, csfm[0:32, 0:1024], op=ALU.mult)
        v.op("tensor_tensor", rps[0:32, 1024:2048], O, csfm[32:64, 0:1024], op=ALU.mult)
        v.op("tensor_tensor", rps[32:64, 0:1024], E, csfm[0:32, 1024:2048], op=ALU.mult)
        v.op("tensor_tensor", rps[32:64, 1024:2048], O, csfm[32:64, 1024:2048], op=ALU.mult)
        v.guard(ssv)
        v.op("tensor_tensor", qpe2[0:32, :], rps[0:32, 0:1024], rps[0:32, 1024:2048],
             op=ALU.subtract)
        v.op("tensor_tensor", qpe2[32:64, :], rps[32:64, 0:1024], rps[32:64, 1024:2048],
             op=ALU.add, inc=s_qe.bump())
        t.wait(s_qe, ctx.sems["qe_" + tag].n)
        qe_base = ctx.sems["qe_" + tag].n
        for jj in range(4):
            bk = 4 + jj % 2
            if jj >= 2:
                t.wait(s_qe, qe_base + jj - 1)
            for c in range(4):
                t.op("matmul", ctx.psum[bk], wkv_[:, c * 256:c * 256 + 128],
                     kvc_fm[:, c * 2048 + jj * 512:(c * 2048) + (jj + 1) * 512],
                     start=(c == 0), stop=(c == 3),
                     inc=(s_q.bump() if c == 3 else None))
            v.wait(s_q, ctx.sems["q_" + tag].n)
            v.op("tensor_copy", kTh[:, jj * 512:(jj + 1) * 512], ctx.psum[bk],
                 inc=s_qe.bump())
        for bg in range(4):
            bk = 4 + bg % 2
            t.wait(s_qe, qe_base + (3 + bg if bg < 2 else 4 + bg - 1))
            for blk in range(4):
                kb = bg * 4 + blk
                for c in range(4):
                    t.op("matmul", ctx.psum[bk][:, blk * 128:(blk + 1) * 128],
                         kvc_fm[:, c * 2048 + kb * 128: c * 2048 + (kb + 1) * 128],
                         wkv_[:, c * 256 + 128:(c + 1) * 256],
                         start=(c == 0), stop=(c == 3),
                         inc=(s_q.bump() if (blk == 3 and c == 3) else None))
            v.wait(s_q, ctx.sems["q_" + tag].n)
            v.op("tensor_copy", vtok[:, bg * 512:(bg + 1) * 512], ctx.psum[bk],
                 inc=s_qe.bump())
        qe_kv = s_qe.n
        for qtr in range(4):
            ncnt = EXT[qtr]
            s0 = qtr * 2
            qc = slice(qtr * 256, (qtr + 1) * 256)
            v.wait(s_xp, ctx.sems["xp_" + tag].n)
            sg0 = s_sg.n
            for kb in range(ncnt):
                sb = kb % 2
                t.wait(s_qe, qe_kv)
                if kb >= 2:
                    t.wait(s_sg, sg0 + 2 * kb - 3)
                t.op("matmul", ctx.psum[sb][:, 0:256], kTh[:, kb * 128:(kb + 1) * 128],
                     qTh[:, qc], start=True, stop=False)
                t.op("matmul", ctx.psum[sb][:, 0:256], kpe_fm[0:64, kb * 128:(kb + 1) * 128],
                     qpe2[0:64, qc], start=False, stop=True, inc=s_sc.bump())
                v.wait(s_sc, ctx.sems["sc_" + tag].n)
                base = 896 + 256 * s0 - 128 * kb
                strip_ap = stripb[:, base:base + 512] \
                    .rearrange("p (m j) -> p m j", j=256)[:, :, 0:128]
                v.op("scalar_tensor_tensor",
                     staged[:, kb * 256:(kb + 1) * 256].rearrange("p (m j) -> p m j", j=128),
                     ctx.psum[sb][:, 0:256].rearrange("p (m j) -> p m j", j=128),
                     1.0, strip_ap, op0=ALU.mult, op1=ALU.add, inc=s_sg.bump())
                v.op("reduce_max", stats[:, kb:kb + 1],
                     staged[:, kb * 256:(kb + 1) * 256], axis=AX.X, inc=s_sg.bump())
            v.wait(s_sg, s_sg.n)
            v.op("reduce_max", stats[:, 16:17], stats[:, 0:ncnt], axis=AX.X)
            v.guard(ssv)
            v.op("engine_nop", inc=s_sg.bump())
            t.wait(s_sg, ctx.sems["sg_" + tag].n)
            t.op("transpose", ctx.psum[7][0:1, 0:128], stats[:, 16:17], pre["identf"],
                 inc=s_av.bump())
            v.wait(s_av, ctx.sems["av_" + tag].n)
            v.op("reduce_max", negMc[0:1, 0:1], ctx.psum[7][0:1, 0:128], axis=AX.X,
                 negate=True)
            v.guard(ssv)
            v.op("engine_nop", inc=s_on.bump())
            t.wait(s_on, ctx.sems["on_" + tag].n)
            t.op("matmul", ctx.psum[7][:, 0:1], pre["onesf_r"], negMc[0:1, 0:1],
                 start=True, stop=True, inc=s_av.bump())
            v.wait(s_av, ctx.sems["av_" + tag].n)
            v.op("tensor_copy", negMc[:, 0:1], ctx.psum[7][:, 0:1])
            v.guard(ssv)
            v.op("engine_nop", inc=s_on.bump())
            sc.wait(s_on, ctx.sems["on_" + tag].n)
            sc.wait(s_av, ctx.sems["av_" + tag].n)
            for kb in range(ncnt):
                sc.op("activation", pT[:, kb * 256:(kb + 1) * 256],
                      staged[:, kb * 256:(kb + 1) * 256], AF.Exp,
                      bias=negMc[:, 0:1], scale=1.0, inc=s_xp.bump())
            xp_now = s_xp.n
            t.wait(s_xp, xp_now)
            for kb in range(ncnt):
                t.op("matmul", ctx.psum[7][0:1, 0:256], pre["ones_c"],
                     pT[:, kb * 256:(kb + 1) * 256], start=(kb == 0), stop=(kb == ncnt - 1))
            for kb in range(ncnt):
                t.op("matmul", ctx.psum[2][:, 0:256], vtok[:, kb * 128:(kb + 1) * 128],
                     pT[:, kb * 256:(kb + 1) * 256], start=(kb == 0), stop=(kb == ncnt - 1),
                     inc=(s_av.bump() if kb == ncnt - 1 else None))
            v.wait(s_av, ctx.sems["av_" + tag].n)
            v.op("reciprocal", srow[0:1, 0:256], ctx.psum[7][0:1, 0:256])
            v.guard(ssv)
            v.op("engine_nop", inc=s_on.bump())
            t.wait(s_on, ctx.sems["on_" + tag].n)
            t.op("matmul", ctx.psum[7][:, 0:256], pre["onesf_r"], srow[0:1, 0:256],
                 start=True, stop=True, inc=s_av.bump())
            v.wait(s_av, ctx.sems["av_" + tag].n)
            v.op("tensor_copy", rdenb, ctx.psum[7][:, 0:256])
            v.guard(ssv)
            v.wait(s_os, ctx.sems["os_" + tag].n)
            v.op("tensor_tensor", oTt, ctx.psum[2][:, 0:256], rdenb, op=ALU.mult,
                 inc=s_on.bump())
            on_now = s_on.n
            q["sync"].wait(s_on, on_now)
            q["sync"].op("dma_start",
                         out=oT_dram[:, h * 1024 + qtr * 256: h * 1024 + (qtr + 1) * 256],
                         in_=oTt, inc=s_os.bump(16))
            t.wait(s_on, on_now)
    os_total = s_os.n
    q_total = s_q.n
    xp_total = s_xp.n
    av_total = s_av.n
    sg_total = s_sg.n
    sc_total = s_sc.n

    if A0PH < 6:
        return s_os
    s_ol = ctx.sem("ol_" + tag)
    s_mm = ctx.sem("mm5_" + tag)
    s_ep = ctx.sem("ep5_" + tag)
    s_xl = ctx.sem("xl5_" + tag)
    s_st = ctx.sem("st5_" + tag)
    q["sync"].wait(s_os, os_total)
    q["sync"].wait(s_q, q_total)
    q["sync"].op("dma_start", out=oT_full, in_=oT_dram[:, :], inc=s_ol.bump(16))
    n_ep = 0
    for og in range(8):
        wb = wo_buf[og % 2]
        if og < 2:
            g_.wait(s_q, q_total)
            g_.wait(s_sc, sc_total)
            g_.wait(s_av, av_total)
            g_.wait(s_xp, xp_total)
        if og >= 2:
            g_.wait(s_mm, (og - 2) * TOKB + TOKB)
        g_.op("dma_start", out=wb.rearrange("p (c o) -> p c o", o=512),
              in_=woT.rearrange("(c p) o -> p c o", c=32)[:, :, og * 512:(og + 1) * 512],
              inc=s_wl.bump(16))
        wl_val = s_wl.n
        for tb in range(TOKB):
            it = og * TOKB + tb
            bk = it % 2
            par = it % 2
            if it < 2:
                q["sync"].wait(s_av, av_total)
                q["sync"].wait(s_xp, xp_total)
            if it >= 2:
                q["sync"].wait(s_ep, it - 1)
            q["sync"].op("dma_start", out=xold[par],
                         in_=x_in[tb * 128:(tb + 1) * 128, og * 512:(og + 1) * 512],
                         inc=s_xl.bump(16))
            t.wait(s_wl, wl_val)
            if it == 0:
                t.wait(s_ol, 16)
            if it >= 2:
                t.wait(s_ep, it - 1)
            for c in range(DCH):
                t.op("matmul", ctx.psum[bk],
                     oT_full[:, c * 1024 + tb * 128: c * 1024 + tb * 128 + 128],
                     wb[:, c * 512:(c + 1) * 512],
                     start=(c == 0), stop=(c == DCH - 1),
                     inc=(s_mm.bump() if c == DCH - 1 else None))
            q["vector"].wait(s_mm, it + 1)
            q["vector"].wait(s_xl, 16 * (it + 1))
            if it >= 2:
                q["vector"].wait(s_st, 16 * (it - 1))
            q["vector"].op("scalar_tensor_tensor", xnew[par], ctx.psum[bk], 1.0, xold[par],
                           op0=ALU.mult, op1=ALU.add, inc=s_ep.bump())
            n_ep += 1
            q["scalar"].wait(s_ep, n_ep)
            q["scalar"].op("dma_start", out=x_out[tb * 128:(tb + 1) * 128, og * 512:(og + 1) * 512],
                           in_=xnew[par], inc=s_st.bump(16))
    return s_st


import math as _math

_PROG_CACHE = {}


def _mkctx(nc):
    ctx = Ctx(nc)
    ctx.setup()
    pre = build_prelude(ctx)
    xn_fm = ctx.sbuf("xn_fm", 32 * 1024, BF16)
    ctx.named["xn_fm_off"] = ctx.sb_off - 65536
    lnbufs = dict(
        xtile=[ctx.sbuf("xt0", 4096, F32)],
        xnbf=[ctx.sbuf("xnb0", 4096, BF16)],
        xsq=None,
        st=[ctx.sbuf(f"st{i}", 8, F32) for i in range(2)],
    )
    ctx.q["tensor"].wait(ctx.sem("pre"), 1)
    ctx.q["vector"].wait(ctx.sem("pre"), 1)
    return ctx, pre, lnbufs


def _prog_ffn0():
    if "ffn0" in _PROG_CACHE:
        return _PROG_CACHE["ffn0"]
    nc = bass.Bass("TRN2", target_bir_lowering=False, debug=False)
    x_in = nc.dram_tensor("x_in", [1024, 4096], F32, kind="ExternalInput").ap()
    y_out = nc.dram_tensor("y_out", [1024, 4096], F32, kind="ExternalOutput").ap()
    w_T = nc.dram_tensor("w_T", [4096, 4096], F32, kind="ExternalInput").ap()
    bias_r = nc.dram_tensor("bias_r", [1, 4096], F32, kind="ExternalInput").ap()
    acat_T = nc.dram_tensor("acat_T", [4096, 48], F32, kind="ExternalInput").ap()
    bcat = nc.dram_tensor("bcat", [48, 4096], F32, kind="ExternalInput").ap()
    lsi = nc.dram_tensor("lsi", [1, 4], I32, kind="ExternalInput").ap()
    bval = nc.dram_tensor("bval", [1, 1], F32, kind="ExternalInput").ap()
    iota3 = nc.dram_tensor("iota3", [1, 3], F32, kind="ExternalInput").ap()
    u_dram = nc.dram_tensor("u_dram", [1, 48], F32, kind="Internal").ap()
    ctx, pre, lnbufs = _mkctx(nc)
    s_st = build_ffn0(ctx, pre, x_in, y_out, w_T, bias_r, acat_T, bcat,
                      lsi, bval, iota3, u_dram, lnbufs, "f0")
    ctx.q["scalar"].wait(s_st, s_st.n)
    ctx.emit()
    ctx.close()
    _PROG_CACHE["ffn0"] = nc
    return nc


def _prog_ffn1():
    if "ffn1" in _PROG_CACHE:
        return _PROG_CACHE["ffn1"]
    nc = bass.Bass("TRN2", target_bir_lowering=False, debug=False)
    x_in = nc.dram_tensor("x_in", [1024, 4096], F32, kind="ExternalInput").ap()
    y_out = nc.dram_tensor("y_out", [1024, 4096], F32, kind="ExternalOutput").ap()
    w_T = nc.dram_tensor("w_T", [4096, 4096], F32, kind="ExternalInput").ap()
    bias_r = nc.dram_tensor("bias_r", [1, 4096], F32, kind="ExternalInput").ap()
    a_T = nc.dram_tensor("a_T", [4096, 64], F32, kind="ExternalInput").ap()
    bflat_T = nc.dram_tensor("bflat_T", [1024, 4096], F32, kind="ExternalInput").ap()
    router_T = nc.dram_tensor("router_T", [4096, 16], F32, kind="ExternalInput").ap()
    ctx, pre, lnbufs = _mkctx(nc)
    s_st = build_ffn1(ctx, pre, x_in, y_out, w_T, bias_r, a_T, bflat_T,
                      router_T, lnbufs, "f1")
    ctx.q["scalar"].wait(s_st, s_st.n)
    ctx.emit()
    ctx.close()
    _PROG_CACHE["ffn1"] = nc
    return nc


def _prog_attn0():
    if "attn0" in _PROG_CACHE:
        return _PROG_CACHE["attn0"]
    nc = bass.Bass("TRN2", target_bir_lowering=False, debug=False)
    x_in = nc.dram_tensor("x_in", [1024, 4096], F32, kind="ExternalInput").ap()
    y_out = nc.dram_tensor("y_out", [1024, 4096], F32, kind="ExternalOutput").ap()
    wqaT = nc.dram_tensor("wqaT", [4096, 1536], F32, kind="ExternalInput").ap()
    qnw_r = nc.dram_tensor("qnw_r", [1, 1536], F32, kind="ExternalInput").ap()
    wqbTp = nc.dram_tensor("wqbTp", [1536, 6144], F32, kind="ExternalInput").ap()
    wkvaT = nc.dram_tensor("wkvaT", [4096, 576], F32, kind="ExternalInput").ap()
    kvnw_r = nc.dram_tensor("kvnw_r", [1, 512], F32, kind="ExternalInput").ap()
    wkvbT = nc.dram_tensor("wkvbT", [512, 8192], F32, kind="ExternalInput").ap()
    woT = nc.dram_tensor("woT", [4096, 4096], F32, kind="ExternalInput").ap()
    strip_c = nc.dram_tensor("strip_c", [128, 2944], F32, kind="ExternalInput").ap()
    cs_own = nc.dram_tensor("cs_own", [1024, 64], F32, kind="ExternalInput").ap()
    cs_fm = nc.dram_tensor("cs_fm", [64, 2048], F32, kind="ExternalInput").ap()
    sh_kv = nc.dram_tensor("sh_kv", [128, 2 * 4608], BF16, kind="Internal",
                           addr_space="Shared").ap()
    b_in = nc.dram_tensor("b_in", [1, 16], F32, kind="Internal").ap()
    b_out = nc.dram_tensor("b_out", [1, 16], F32, kind="Internal").ap()
    oT_dram = nc.dram_tensor("oT_dram", [128, 32768], BF16, kind="Internal").ap()
    ctx, pre, lnbufs = _mkctx(nc)
    s_st = build_attn0(ctx, pre, x_in, y_out, wqaT, qnw_r, wqbTp, wkvaT,
                       kvnw_r, wkvbT, woT, strip_c, cs_own, cs_fm,
                       sh_kv, b_in, b_out, oT_dram, lnbufs, "a0")
    ctx.q["scalar"].wait(s_st, s_st.n)
    ctx.emit()
    ctx.close()
    _PROG_CACHE["attn0"] = nc
    return nc


def _prog_attn1():
    if "attn1" in _PROG_CACHE:
        return _PROG_CACHE["attn1"]
    nc = bass.Bass("TRN2", target_bir_lowering=False, debug=False)
    x_in = nc.dram_tensor("x_in", [1024, 4096], F32, kind="ExternalInput").ap()
    y_out = nc.dram_tensor("y_out", [1024, 4096], F32, kind="ExternalOutput").ap()
    wqkv = nc.dram_tensor("wqkv", [4096, 6144], F32, kind="ExternalInput").ap()
    wo = nc.dram_tensor("wo", [4096, 4096], F32, kind="ExternalInput").ap()
    strip_h = nc.dram_tensor("strip_h", [4096, 2944], F32, kind="ExternalInput").ap()
    sh_ex = nc.dram_tensor("sh_ex", [128, 2 * 16384], BF16, kind="Internal",
                           addr_space="Shared").ap()
    b_in = nc.dram_tensor("b_in", [1, 16], F32, kind="Internal").ap()
    b_out = nc.dram_tensor("b_out", [1, 16], F32, kind="Internal").ap()
    oT_dram = nc.dram_tensor("oT_dram", [128, 32768], BF16, kind="Internal").ap()
    ctx, pre, lnbufs = _mkctx(nc)
    s_st = build_attn1(ctx, pre, x_in, y_out, wqkv, wo, strip_h, sh_ex, b_in, b_out,
                       oT_dram, lnbufs, "a1")
    ctx.q["scalar"].wait(s_st, s_st.n)
    ctx.emit()
    ctx.close()
    _PROG_CACHE["attn1"] = nc
    return nc


def _alibi_slopes(n):
    cp2 = 2 ** int(_math.floor(_math.log2(n)))
    base = 2.0 ** (-8.0 / cp2)
    slopes = base ** np.arange(1, cp2 + 1, dtype=np.float32)
    if cp2 != n:
        eb = 2.0 ** (-4.0 / cp2)
        extra = eb ** np.arange(1, 1 + 2 * (n - cp2), 2, dtype=np.float32)
        slopes = np.concatenate([slopes, extra])
    return slopes.astype(np.float32)


def _own_rows(j):
    return np.concatenate([np.arange((2 * s + j) * 128, (2 * s + j) * 128 + 128)
                           for s in range(8)])


def kernel(**inp):
    f32 = np.float32
    X = np.asarray(inp["L0_attn_x"], f32)

    l0f_wT = np.ascontiguousarray(np.asarray(inp["L0_ffn_weight"], f32).T)
    l0f_bias = np.asarray(inp["L0_ffn_bias"], f32).reshape(1, 4096)
    lA = np.asarray(inp["L0_ffn_lora_A"], f32)
    lB = np.asarray(inp["L0_ffn_lora_B"], f32)
    l0f_acatT = np.ascontiguousarray(lA.reshape(48, 4096).T)
    l0f_bcat = np.ascontiguousarray(np.concatenate([lB[k].T for k in range(3)], 0))
    lsi_r = np.asarray(inp["L0_ffn_lora_start_indices"]).astype(np.int32).reshape(1, 4)
    iota3 = np.arange(3, dtype=f32).reshape(1, 3)

    a0_wqaT = np.ascontiguousarray(np.asarray(inp["L0_attn_wq_a"], f32).T)
    a0_qnw = np.asarray(inp["L0_attn_q_norm_w"], f32).reshape(1, 1536)
    Wqb = np.asarray(inp["L0_attn_wq_b"], f32)
    perm = np.concatenate([np.concatenate([h * 192 + np.arange(128),
                                           h * 192 + 128 + 2 * np.arange(32),
                                           h * 192 + 129 + 2 * np.arange(32)])
                           for h in range(32)])
    a0_wqbTp = np.ascontiguousarray(Wqb.T[:, perm])
    a0_wkvaT = np.ascontiguousarray(np.asarray(inp["L0_attn_wkv_a"], f32).T)
    a0_kvnw = np.asarray(inp["L0_attn_kv_norm_w"], f32).reshape(1, 512)
    a0_wkvbT = np.ascontiguousarray(np.asarray(inp["L0_attn_wkv_b"], f32).T)
    a0_woT = np.ascontiguousarray(np.asarray(inp["L0_attn_wo"], f32).T)

    l1f_wT = np.ascontiguousarray(np.asarray(inp["L1_ffn_base_w"], f32).T)
    l1f_bias = np.asarray(inp["L1_ffn_base_b"], f32).reshape(1, 4096)
    l1f_aT = np.ascontiguousarray(np.asarray(inp["L1_ffn_A"], f32).T)
    Bm = np.asarray(inp["L1_ffn_B"], f32)
    l1f_bflatT = np.ascontiguousarray(Bm.transpose(0, 2, 1).reshape(1024, 4096))
    l1f_routerT = np.ascontiguousarray(np.asarray(inp["L1_ffn_router_w"], f32).T)

    a1_wqkv = np.ascontiguousarray(np.asarray(inp["L1_attn_w_qkv"], f32))
    a1_wo = np.ascontiguousarray(np.asarray(inp["L1_attn_w_o"], f32))

    QR = 64
    freqs = 1.0 / 10000.0 ** (np.arange(0, QR, 2, dtype=f32) / QR)
    ang = np.outer(np.arange(2048, dtype=f32), freqs)
    COS, SIN = np.cos(ang).astype(f32), np.sin(ang).astype(f32)
    SW = 2944
    strips_c, strips_h = [], []
    SL = _alibi_slopes(32)
    for par in range(2):
        d = np.arange(SW)[None, :] - np.arange(128)[:, None] + (128 * par - 896)
        strips_c.append(np.where(d >= 0, 0.0, -1e30).astype(f32))
        sq = np.sqrt(np.maximum(d, 0).astype(f32))
        sh = np.zeros((32, 128, SW), f32)
        for h in range(32):
            sh[h] = np.where(d >= 0, -SL[h] * sq, -1e30)
        strips_h.append(sh.reshape(32 * 128, SW))
    cs_own_j, cs_fm_j = [], []
    for j in range(2):
        rows = _own_rows(j)
        cs_own_j.append(np.ascontiguousarray(
            np.concatenate([COS[rows], SIN[rows]], axis=1)))
        cs_fm_j.append(np.ascontiguousarray(
            np.block([[COS[rows].T, SIN[rows].T], [SIN[rows].T, COS[rows].T]])))

    xs = []
    for core in range(8):
        b, j = core // 2, core % 2
        xs.append(np.ascontiguousarray(X[b][_own_rows(j)]))

    cores = list(range(8))

    def run(nc, maps):
        res = run_bass_kernel_spmd(nc, maps, core_ids=cores)
        return [np.asarray(r["y_out"]) for r in res.results]

    def ffn0_maps(xs_):
        return [{"x_in": xs_[c], "w_T": l0f_wT, "bias_r": l0f_bias,
                 "acat_T": l0f_acatT, "bcat": l0f_bcat, "lsi": lsi_r,
                 "bval": np.array([[float(c // 2)]], f32), "iota3": iota3}
                for c in cores]

    def ffn1_maps(xs_):
        return [{"x_in": xs_[c], "w_T": l1f_wT, "bias_r": l1f_bias,
                 "a_T": l1f_aT, "bflat_T": l1f_bflatT, "router_T": l1f_routerT}
                for c in cores]

    def attn0_maps(xs_):
        return [{"x_in": xs_[c], "wqaT": a0_wqaT, "qnw_r": a0_qnw,
                 "wqbTp": a0_wqbTp, "wkvaT": a0_wkvaT, "kvnw_r": a0_kvnw,
                 "wkvbT": a0_wkvbT, "woT": a0_woT, "strip_c": strips_c[c % 2],
                 "cs_own": cs_own_j[c % 2], "cs_fm": cs_fm_j[c % 2]}
                for c in cores]

    def attn1_maps(xs_):
        return [{"x_in": xs_[c], "wqkv": a1_wqkv, "wo": a1_wo,
                 "strip_h": strips_h[c % 2]} for c in cores]

    xs = run(_prog_ffn0(), ffn0_maps(xs))
    xs = run(_prog_attn0(), attn0_maps(xs))
    xs = run(_prog_ffn0(), ffn0_maps(xs))
    xs = run(_prog_ffn1(), ffn1_maps(xs))
    xs = run(_prog_attn1(), attn1_maps(xs))
    xs = run(_prog_ffn1(), ffn1_maps(xs))

    out = np.zeros((4, 2048, 4096), np.float32)
    for core in cores:
        b, j = core // 2, core % 2
        out[b][_own_rows(j)] = xs[core]
    return out
